# version 17 (final)
# baseline (speedup 1.0000x reference)
import math
import os
import numpy as np
import ml_dtypes
import concourse.bass as bass
import concourse.mybir as mybir
from concourse.bass_utils import run_bass_kernel_spmd

F32 = mybir.dt.float32
BF16 = mybir.dt.bfloat16
I32 = mybir.dt.int32
AF = mybir.ActivationFunctionType
ALU = mybir.AluOpType
AX = mybir.AxisListType
BF = ml_dtypes.bfloat16

ENGINES = ("sync", "tensor", "scalar", "vector", "gpsimd")
EPS = 1e-5
MBASE = 20480
MTOP = 229344


class Cfg:
    def __init__(self, NT, DC, HN, ICN):
        self.NT = NT
        self.QB = NT // 4
        self.W = self.QB * 128
        self.DC = DC
        self.HN = HN
        self.ICN = ICN
        self.S = NT * 128
        self.E = DC * 128
        self.I = ICN * 128
        self.TOWN = 2 * self.W
        self.NOWN = 2 * self.QB
        self.CR = (NT - 1) * 128
        self.RW = self.CR + self.W
        self.NKT = [2 * self.QB, NT]
        self.SCALING = 128.0 ** -0.5

    def own_qts(self, parity):
        QB = self.QB
        if parity == 0:
            return list(range(0, QB)) + list(range(3 * QB, 4 * QB))
        return list(range(QB, 3 * QB))


def alibi_slopes(n):
    cp2 = 2 ** int(math.floor(math.log2(n)))
    base = 2.0 ** (-(2.0 ** (-(math.log2(cp2) - 3))))
    slopes = base ** np.arange(1, cp2 + 1, dtype=np.float64)
    if cp2 != n:
        eb = 2.0 ** (-(2.0 ** (-(math.log2(2 * cp2) - 3))))
        rem = min(cp2, n - cp2)
        slopes = np.concatenate([slopes, eb ** np.arange(1, 1 + 2 * rem, 2, dtype=np.float64)])
    return slopes.astype(np.float64)


def make_rbase(cfg, parity):
    qts = cfg.own_qts(parity)
    out = np.zeros((128, 2 * cfg.RW), dtype=np.float32)
    p = np.arange(128)[:, None]
    g = np.arange(cfg.RW)[None, :]
    for qb in range(2):
        qbbase = qts[qb * cfg.QB] * 128
        delta = p + cfg.CR - g - qbbase
        out[:, qb * cfg.RW:(qb + 1) * cfg.RW] = np.where(
            delta <= 0, delta.astype(np.float32), np.float32(-1e30))
    return out


class Emit:
    def __init__(self):
        self.q = {e: [] for e in ENGINES}
        self.phases = []
        self.c = {}
        self.sems = {}
        self.nc = None

    def sem(self, name):
        if name not in self.sems:
            self.sems[name] = self.nc.semaphore(name).__enter__()
            self.c[name] = 0
        return self.sems[name]

    def make_sems(self, nc, names):
        self.nc = nc
        for n in names:
            self.sem(n)

    def dma(self, eng, stream, out_ap, in_ap):
        self.sem(stream)
        self.wait(eng, stream, self.c[stream])
        return self.op(eng, lambda E, o=out_ap, i=in_ap: E.dma_start(out=o, in_=i),
                       inc=stream, by=16)

    def op(self, eng, make, inc=None, by=1):
        v = None
        if inc is not None:
            sem = self.sem(inc)
            self.c[inc] += by
            v = self.c[inc]

            def fn(E, make=make, sem=sem, by=by):
                make(E).then_inc(sem, by)
        else:
            def fn(E, make=make):
                make(E)
        self.q[eng].append(fn)
        return v

    def wait(self, eng, sem_name, val):
        if val is None or val <= 0:
            return
        sem = self.sem(sem_name)
        self.q[eng].append(lambda E, s=sem, v=val: E.wait_ge(s, v))

    def phase(self, name):
        self.phases.append((name, getattr(self, "n_mm", 0)))

    def barrier(self):
        snap = dict(self.c)
        for e in ENGINES:
            for s, v in snap.items():
                self.wait(e, s, v)


def build_program(cfg):
    nc = bass.Bass("TRN2", target_bir_lowering=False, debug=False)
    NT, QB, W, DC, HN, ICN = cfg.NT, cfg.QB, cfg.W, cfg.DC, cfg.HN, cfg.ICN
    E, S, TOWN, NOWN, RW, CR = cfg.E, cfg.S, cfg.TOWN, cfg.NOWN, cfg.RW, cfg.CR
    OCN = DC
    E_DIM = float(E)

    a_xb = nc.dram_tensor("xb", [S, E], F32, kind="ExternalInput").ap()
    a_xo = nc.dram_tensor("xown", [TOWN, E], F32, kind="ExternalInput").ap()
    a_wqkv = nc.dram_tensor("wqkvT", [HN + 2, 128, DC, 128], F32, kind="ExternalInput").ap()
    a_wc = nc.dram_tensor("wcT", [OCN, 128, HN, 128], F32, kind="ExternalInput").ap()
    a_gw = nc.dram_tensor("gwT", [ICN, 128, DC, 128], I32, kind="ExternalInput").ap()
    a_uw = nc.dram_tensor("uwT", [ICN, 128, DC, 128], I32, kind="ExternalInput").ap()
    a_dw = nc.dram_tensor("dwT", [OCN, 128, ICN, 128], I32, kind="ExternalInput").ap()
    a_rb = nc.dram_tensor("rbase", [128, 2 * RW], F32, kind="ExternalInput").ap()
    a_sc = nc.dram_tensor("scal3", [128, 3], F32, kind="ExternalInput").ap()
    a_out = nc.dram_tensor("out", [TOWN, E], F32, kind="ExternalOutput").ap()
    d_x1 = nc.dram_tensor("x1s", [TOWN, E], BF16, kind="Internal").ap()
    a_idf = nc.inline_tensor(np.eye(128, dtype=np.float32), name="ident_f32").ap()
    a_mcf = nc.inline_tensor(np.full((128, 1), 1.0 / E, dtype=np.float32),
                             name="mcol_f32").ap()

    em = Emit()

    ident = nc.alloc_sbuf_tensor("ident", [128, 128], BF16)
    mcol = nc.alloc_sbuf_tensor("mcol", [128, 1], BF16)
    sc3 = nc.alloc_sbuf_tensor("sc3", [128, 3], F32)
    usds = nc.alloc_sbuf_tensor("usds", [128, 1], F32)
    sm8 = nc.alloc_sbuf_tensor("sm8", [128, 8], F32)
    sq8 = nc.alloc_sbuf_tensor("sq8", [128, 8], F32)
    smt = nc.alloc_sbuf_tensor("smt", [128, 1], F32)
    sqt = nc.alloc_sbuf_tensor("sqt", [128, 1], F32)
    m_t = nc.alloc_sbuf_tensor("m_t", [128, 1], F32)
    v_t = nc.alloc_sbuf_tensor("v_t", [128, 1], F32)
    tt1 = nc.alloc_sbuf_tensor("tt1", [128, 1], F32)
    rstd = nc.alloc_sbuf_tensor("rstd", [128, 1], F32)
    negms = nc.alloc_sbuf_tensor("negms", [128, 1], F32)
    rcp = nc.alloc_sbuf_tensor("rcp", [128, 1], F32)
    sm2 = nc.alloc_sbuf_tensor("sm2", [128, 2], F32)
    sq2 = nc.alloc_sbuf_tensor("sq2", [128, 2], F32)
    m2 = nc.alloc_sbuf_tensor("m2", [128, 2], F32)
    v2 = nc.alloc_sbuf_tensor("v2", [128, 2], F32)
    t2 = nc.alloc_sbuf_tensor("t2", [128, 2], F32)
    rstd2 = nc.alloc_sbuf_tensor("rstd2", [128, 2], F32)
    negms2 = nc.alloc_sbuf_tensor("negms2", [128, 2], F32)
    hg = [nc.alloc_sbuf_tensor(f"hg{i}", [128, W], BF16) for i in range(2)]

    EB, EF = E * 2, E * 4
    lay = {}

    def at(name, off, shape, dtype):
        nbytes = int(np.prod(shape[1:])) * (2 if dtype == BF16 else 4)
        assert MBASE + off + nbytes <= MTOP, (name, off, nbytes, MBASE + off + nbytes)
        lay[name] = (off, off + nbytes)
        return nc.alloc_sbuf_tensor_at(name, shape, dtype, offset=MBASE + off)

    o = 0
    kT = at("kT", o, [128, S], BF16); o += S * 2
    vx = at("vx", o, [128, NT * 132], BF16); o += NT * 132 * 2
    oA = o
    xsA = at("xsA", oA, [128, E], F32)
    xsB = at("xsB", oA + EF, [128, E], F32)
    xn = [at(f"xn{i}", oA + 2 * EF + i * EB, [128, E], BF16) for i in range(2)]
    xo = [at(f"xo{i}", oA + 2 * EF + (2 + i) * EB, [128, E], BF16) for i in range(2)]
    oX2 = oA
    x2T = at("x2T", oX2, [128, DC * TOWN], BF16)
    oOT = oX2 + max(DC * TOWN * 2, 2 * EF + 4 * EB)
    REG2 = max(HN * TOWN * 2,
               2 * EB,
               2 * EF + 2 * EB,
               3 * EF + 2 * EB,
               4 * EB,
               2 * cfg.I * 2,
               2 * EB + 2 * EF)
    wkv = [at(f"wkv{i}", oOT + i * EB, [128, E], BF16) for i in range(2)]
    oT = at("oT", oOT, [128, HN * TOWN], BF16)
    oP2a = oOT
    xs2 = [at(f"xs2_{i}", oP2a + i * EF, [128, E], F32) for i in range(2)]
    xn2 = [at(f"xn2_{i}", oP2a + 2 * EF + i * EB, [128, E], BF16) for i in range(2)]
    oRH = oOT + REG2
    rh = at("rh", oRH, [128, 2 * RW], F32)
    oWQ = oRH + 2 * RW * 4
    wq = [at(f"wq{i}", oWQ + i * EB, [128, E], BF16) for i in range(2)]
    oQH = oWQ + 2 * EB
    qh = [at(f"qh{i}", oQH + i * TOWN * 2, [128, TOWN], BF16) for i in range(2)]
    oPT = oQH + 2 * TOWN * 2
    pT = at("pT", oPT, [128, NT * W], BF16)
    oOSB = oPT + NT * W * 2
    osb = [at(f"osb{i}", oOSB + i * 256, [128, 128], BF16) for i in range(2)]
    assert oOSB + 512 <= MTOP - MBASE
    xtT = at("xtT", oA, [128, OCN * TOWN], BF16)
    wcb = [at(f"wcb{i}", oRH + i * HN * 256, [128, HN * 128], BF16) for i in range(2)]
    x4T = [at(f"x4T{i}", oRH + i * DC * W * 2, [128, DC * W], BF16) for i in range(2)]
    oP4t = oOT
    xs4 = [at(f"xs4_{i}", oP4t + i * EF, [128, E], F32) for i in range(2)]
    t1 = at("t1", oP4t + 2 * EF, [128, E], BF16)
    x1f = at("x1f", oP4t + 2 * EF + EB, [128, E], F32)
    x1b = at("x1b", oP4t + 3 * EF + EB, [128, E], BF16)
    hT = at("hT", 0, [128, ICN * W], BF16)
    hT_end = ICN * W * 2
    assert hT_end <= oRH, ("hT overlaps x4T/x5T", hT_end, oRH)
    oW5 = max(oOT, hT_end)
    gwb = [at(f"gwb{i}", oW5 + i * EB, [128, E], BF16) for i in range(2)]
    uwb = [at(f"uwb{i}", oW5 + (2 + i) * EB, [128, E], BF16) for i in range(2)]
    dwb = [at(f"dwb{i}", oW5 + i * cfg.I * 2, [128, cfg.I], BF16) for i in range(2)]
    x5T = [at(f"x5T{i}", oRH + i * DC * W * 2, [128, OCN * W], BF16) for i in range(2)]
    t5 = at("t5", oW5, [128, E], BF16)
    x1r = at("x1r", oW5 + EB, [128, E], BF16)
    outb = [at(f"outb{i}", oW5 + 2 * EB + i * EF, [128, E], F32) for i in range(2)]
    assert oW5 + 2 * EB + 2 * EF <= oRH, "P6 scratch overlaps x4T/x5T"
    assert oW5 + 2 * cfg.I * 2 <= oRH, "dwb overlaps x4T/x5T"

    pb = [nc.alloc_psum_tensor(f"pb{j}", [128, 512], F32) for j in range(8)]

    em.make_sems(nc, ["pe", "act", "dve"])

    vx3 = vx[:].rearrange("p (t c) -> p t c", c=132)
    x2T3 = x2T[:].rearrange("p (d t) -> p d t", t=TOWN)
    oT3 = oT[:].rearrange("p (h t) -> p h t", t=TOWN)
    xtT3 = xtT[:].rearrange("p (o t) -> p o t", t=TOWN)
    pT3 = pT[:].rearrange("p (k w) -> p k w", w=W)
    hT3 = hT[:].rearrange("p (i w) -> p i w", w=W)
    x4T3 = [x4T[i][:].rearrange("p (d t) -> p d t", t=W) for i in range(2)]
    x5T3 = [x5T[i][:].rearrange("p (o t) -> p o t", t=W) for i in range(2)]

    def r128(t):
        return t[:].rearrange("p (a f) -> p a f", f=128)


    def mm(out_ap, lhsT_ap, rhs_ap, start, stop, inc=None):
        em.n_mm = getattr(em, "n_mm", 0) + 1
        return em.op("tensor",
                     lambda E, o=out_ap, l=lhsT_ap, r=rhs_ap, s1=start, s2=stop:
                     E.matmul(o, l, r, start=s1, stop=s2, skip_group_check=True),
                     inc=inc)

    em.dma("gpsimd", "cst", ident[:], a_idf)
    em.dma("gpsimd", "cst", mcol[:], a_mcf)
    em.dma("sync", "csl", sc3[:], a_sc)
    v_consts_ld = em.dma("sync", "csl", rh[:], a_rb)
    em.wait("vector", "csl", v_consts_ld)
    em.op("vector", lambda E: E.tensor_tensor(usds[:], sc3[:, 1:2], sc3[:, 2:3], ALU.mult),
          inc="dve")
    v_vxones = None
    for t in range(NT):
        v_vxones = em.op("vector", lambda E, t=t: E.memset(vx3[:, t, 128:129], 1.0),
                         inc="dve")
    em.dma("gpsimd", "cst", r128(wkv[0]), a_wqkv[HN])
    v_gw_kv = em.dma("gpsimd", "cst", r128(wkv[1]), a_wqkv[HN + 1])

    def ln_stats_apply(src, dst, vpre=None):
        v1 = em.op("scalar", lambda E: E.activation(dst[:], src[:], AF.Square,
                                                    accum_out=sqt[:]), inc="act")
        if True:
            if vpre is not None:
                em.wait("vector", vpre[0], vpre[1])
            w0 = em.op("vector", lambda E: E.reduce_sum(smt[:], src[:], axis=AX.X),
                       inc="dve")
            v_a = v1
        else:
            em.wait("scalar", "act", v1)
            v_a = em.op("scalar", lambda E: E.activation(dst[:], src[:], AF.Copy,
                                                         accum_out=smt[:]), inc="act")
            w0 = 0
        em.wait("vector", "act", v_a)
        em.wait("vector", "dve", w0)
        w1 = em.op("vector", lambda E: E.tensor_scalar(m_t[:], smt[:], 1.0 / E_DIM, None,
                                                       ALU.mult), inc="dve")
        w2 = em.op("vector", lambda E: E.tensor_scalar(v_t[:], sqt[:], 1.0 / E_DIM, None,
                                                       ALU.mult), inc="dve")
        em.wait("vector", "dve", w1)
        w3 = em.op("vector", lambda E: E.tensor_tensor(tt1[:], m_t[:], m_t[:], ALU.mult),
                   inc="dve")
        em.wait("vector", "dve", max(w2, w3))
        w4 = em.op("vector", lambda E: E.tensor_tensor(v_t[:], v_t[:], tt1[:], ALU.subtract),
                   inc="dve")
        em.wait("vector", "dve", w4)
        v_b = em.op("vector", lambda E: E.tensor_scalar(v_t[:], v_t[:], float(EPS), None,
                                                        ALU.add), inc="dve")
        em.wait("scalar", "dve", v_b)
        u1 = em.op("scalar", lambda E: E.activation(tt1[:], v_t[:], AF.Ln), inc="act")
        em.wait("scalar", "act", u1)
        v_c = em.op("scalar", lambda E: E.activation(rstd[:], tt1[:], AF.Exp, scale=-0.5),
                    inc="act")
        em.wait("vector", "act", v_c)
        w5 = em.op("vector", lambda E: E.tensor_scalar(tt1[:], m_t[:], -1.0, None, ALU.mult),
                   inc="dve")
        em.wait("vector", "dve", w5)
        v_d = em.op("vector", lambda E: E.tensor_tensor(negms[:], tt1[:], rstd[:], ALU.mult),
                    inc="dve")
        em.wait("scalar", "dve", v_d)
        v_ap = em.op("scalar", lambda E: E.activation(dst[:], src[:], AF.Identity,
                                                      scale=rstd[:], bias=negms[:]), inc="act")
        return v_ap, w0

    def ln0_pair_emit(srcA_fn, srcB_fn, xs_t, xn_t, evacA_fn, evacB_fn, v_state,
                      streams):
        vx_ld = [None, None]
        for i in range(2):
            em.wait("sync", "act", v_state["xs_free"][i])
            em.wait("sync", "dve", v_state["xs_dve"][i])
            vx_ld[i] = em.dma("sync", streams[i], xs_t[i][:],
                              (srcA_fn if i == 0 else srcB_fn)())
        v1 = None
        for i in range(2):
            em.wait("scalar", streams[i], vx_ld[i])
            em.wait("scalar", "pe", v_state["xn_free"][i])
            v1 = em.op("scalar", lambda E, i=i: E.activation(
                xn_t[i][:], xs_t[i][:], AF.Square, accum_out=sq2[:, i:i + 1]),
                inc="act")
        w0 = None
        for i in range(2):
            em.wait("vector", streams[i], vx_ld[i])
            w0 = em.op("vector", lambda E, i=i: E.reduce_sum(
                sm2[:, i:i + 1], xs_t[i][:], axis=AX.X), inc="dve")
            v_state["xs_dve"][i] = w0
        em.wait("vector", "act", v1)
        em.wait("vector", "dve", w0)
        wa = em.op("vector", lambda E: E.tensor_scalar(m2[:], sm2[:], 1.0 / E_DIM, None,
                                                       ALU.mult), inc="dve")
        wb = em.op("vector", lambda E: E.tensor_scalar(v2[:], sq2[:], 1.0 / E_DIM, None,
                                                       ALU.mult), inc="dve")
        em.wait("vector", "dve", wa)
        wc_ = em.op("vector", lambda E: E.tensor_tensor(t2[:], m2[:], m2[:], ALU.mult),
                    inc="dve")
        em.wait("vector", "dve", max(wb, wc_))
        wd = em.op("vector", lambda E: E.tensor_tensor(v2[:], v2[:], t2[:], ALU.subtract),
                   inc="dve")
        em.wait("vector", "dve", wd)
        we = em.op("vector", lambda E: E.tensor_scalar(v2[:], v2[:], float(EPS), None,
                                                       ALU.add), inc="dve")
        em.wait("scalar", "dve", we)
        u1 = em.op("scalar", lambda E: E.activation(t2[:], v2[:], AF.Ln), inc="act")
        em.wait("scalar", "act", u1)
        u2 = em.op("scalar", lambda E: E.activation(rstd2[:], t2[:], AF.Exp, scale=-0.5),
                   inc="act")
        em.wait("vector", "act", u2)
        wf = em.op("vector", lambda E: E.tensor_scalar(t2[:], m2[:], -1.0, None, ALU.mult),
                   inc="dve")
        em.wait("vector", "dve", wf)
        wg = em.op("vector", lambda E: E.tensor_tensor(negms2[:], t2[:], rstd2[:],
                                                       ALU.mult), inc="dve")
        em.wait("scalar", "dve", wg)
        for i in range(2):
            v_ap = em.op("scalar", lambda E, i=i: E.activation(
                xn_t[i][:], xs_t[i][:], AF.Identity,
                scale=rstd2[:, i:i + 1], bias=negms2[:, i:i + 1]), inc="act")
            v_state["xs_free"][i] = v_ap
            em.wait("tensor", "act", v_ap)
            evac_fn = evacA_fn if i == 0 else evacB_fn
            for q in range(DC // 4):
                bk = q % 2
                em.wait("tensor", "dve", v_state["tp_free"][bk])
                vq = None
                for j in range(4):
                    dc = q * 4 + j
                    vq = mm(pb[bk][:, j * 128:(j + 1) * 128],
                            xn_t[i][:, dc * 128:(dc + 1) * 128], ident[:],
                            True, True, inc=("pe" if j == 3 else None))
                em.wait("vector", "pe", vq)
                em.wait("vector", "pe", v_state["dst_free"])
                v_state["tp_free"][bk] = em.op(
                    "vector",
                    lambda E, q=q, bk=bk, out=evac_fn(q):
                    E.tensor_copy(out, pb[bk][:].rearrange("p (d f) -> p d f", f=128)),
                    inc="dve")
            v_state["xn_free"][i] = em.c["pe"]

    em.phase('P1')
    em.wait("tensor", "cst", v_gw_kv)
    em.wait("tensor", "dve", v_vxones)
    st1 = {"xs_free": [0, 0], "xn_free": [0, 0], "tp_free": [0, 0], "dst_free": 0,
           "xs_dve": [0, 0]}
    v_kv_evac = [0, 0]
    xs_pair = [xsA, xsB]
    xo_free = [0, 0]

    def kv_tile(tc, bi):
        em.wait("tensor", "dve", st1["tp_free"][0])
        em.wait("tensor", "dve", st1["tp_free"][1])
        em.wait("tensor", "dve", v_kv_evac[0])
        vk = None
        for dc in range(DC):
            vk = mm(pb[4][:, 0:128], wkv[0][:, dc * 128:(dc + 1) * 128],
                    r128(xo[bi])[:, dc, :], dc == 0, dc == DC - 1,
                    inc=("pe" if dc == DC - 1 else None))
        em.wait("vector", "pe", vk)
        v_kv_evac[0] = em.op("vector", lambda E, tc=tc: E.tensor_copy(
            kT[:, tc * 128:(tc + 1) * 128], pb[4][:, 0:128]), inc="dve")
        em.wait("tensor", "dve", v_kv_evac[1])
        vv = None
        for dc in range(DC):
            vv = mm(pb[5][:, 0:128], r128(xo[bi])[:, dc, :],
                    wkv[1][:, dc * 128:(dc + 1) * 128], dc == 0, dc == DC - 1,
                    inc=("pe" if dc == DC - 1 else None))
        em.wait("vector", "pe", vv)
        v_kv_evac[1] = em.op("vector", lambda E, tc=tc: E.tensor_copy(
            vx3[:, tc, 0:128], pb[5][:, 0:128]), inc="dve")
        xo_free[bi] = em.c["pe"]

    for tp in range(NT // 2):
        tA, tB = 2 * tp, 2 * tp + 1
        st1["dst_free"] = max(xo_free)
        ln0_pair_emit(lambda tA=tA: a_xb[tA * 128:(tA + 1) * 128, :],
                      lambda tB=tB: a_xb[tB * 128:(tB + 1) * 128, :],
                      xs_pair, xn,
                      lambda q: r128(xo[0])[:, q * 4:q * 4 + 4, :],
                      lambda q: r128(xo[1])[:, q * 4:q * 4 + 4, :],
                      st1, ["xs0", "xs1"])
        kv_tile(tA, 0)
        kv_tile(tB, 1)

    em.barrier()

    em.phase('P2a')
    st2 = {"xs_free": [0, 0], "xn_free": [0, 0], "tp_free": [0, 0], "dst_free": 0,
           "xs_dve": [0, 0]}
    for tp in range(NOWN // 2):
        tA, tB = 2 * tp, 2 * tp + 1
        ln0_pair_emit(lambda tA=tA: a_xo[tA * 128:(tA + 1) * 128, :],
                      lambda tB=tB: a_xo[tB * 128:(tB + 1) * 128, :],
                      xs2, xn2,
                      lambda q, tA=tA: x2T3[:, q * 4:q * 4 + 4, tA * 128:(tA + 1) * 128],
                      lambda q, tB=tB: x2T3[:, q * 4:q * 4 + 4, tB * 128:(tB + 1) * 128],
                      st2, ["x20", "x21"])

    em.barrier()

    em.phase('P2b')
    slopes = alibi_slopes(HN)
    qlo_union = [0, 2 * QB * 128]

    def kept_kts(h, qb):
        return [kt for kt in range(cfg.NKT[qb])
                if not slopes[h] * (qlo_union[qb] - (kt * 128 + 127)) > 145.0]

    v_wq_free = [0, 0]
    v_qh_free = [0, 0]
    v_q_evac = [0, 0]
    v_s_free = [0, 0, 0, 0]
    v_osb_free = [0, 0]
    v_sb_evfree = [0, 0, 0, 0]
    v_pt_free = 0

    def q_mms(h, vq_last):
        hb = h % 2
        em.wait("gpsimd", "pe", v_wq_free[hb])
        v_w = em.dma("gpsimd", f"wq{hb}", r128(wq[hb]), a_wqkv[h])
        em.wait("tensor", f"wq{hb}", v_w)
        for qb in range(2):
            em.wait("tensor", "dve", v_q_evac[qb])
            for dc in range(DC):
                vq_last[qb] = mm(pb[qb][:, 0:W], wq[hb][:, dc * 128:(dc + 1) * 128],
                                 x2T3[:, dc, qb * W:(qb + 1) * W], dc == 0, dc == DC - 1,
                                 inc=("pe" if dc == DC - 1 else None))
                yield

    def q_evacs(h, vq_last):
        hb = h % 2
        for qb in range(2):
            em.wait("vector", "pe", vq_last[qb])
            em.wait("vector", "pe", v_qh_free[hb])
            v_q_evac[qb] = em.op("vector", lambda E, qb=qb, hb=hb: E.tensor_copy(
                qh[hb][:, qb * W:(qb + 1) * W], pb[qb][:, 0:W]), inc="dve")
        v_wq_free[hb] = em.c["pe"]

    vq0 = [None, None]
    for _ in q_mms(0, vq0):
        pass
    q_evacs(0, vq0)

    for h in range(HN):
        hb = h % 2
        ratio = (slopes[0] / cfg.SCALING) if h == 0 else (slopes[h] / slopes[h - 1])
        v_rh = em.op("vector", lambda E, r=float(ratio): E.tensor_scalar(
            rh[:], rh[:], r, None, ALU.mult), inc="dve")
        em.wait("vector", "dve", v_rh)
        em.wait("scalar", "pe", v_pt_free)
        if h + 1 < HN:
            vq_next = [None, None]
            qgen = q_mms(h + 1, vq_next)
        else:
            qgen = None
        for qb in range(2):
            kts = kept_kts(h, qb)
            em.wait("tensor", "dve", v_q_evac[qb])
            for i_kt, kt in enumerate(kts):
                kb = i_kt % 4
                sb = 2 + kb
                em.wait("tensor", "act", v_s_free[kb])
                em.wait("tensor", "dve", v_sb_evfree[kb])
                v_s = mm(pb[sb][:, 0:W], kT[:, kt * 128:(kt + 1) * 128],
                         qh[hb][:, qb * W:(qb + 1) * W], True, True, inc="pe")
                em.wait("vector", "pe", v_s)
                off = qb * RW + CR - 128 * kt
                v_b = em.op("vector", lambda E, sb=sb, off=off: E.tensor_tensor(
                    pb[sb][:, 0:W], pb[sb][:, 0:W], rh[:, off:off + W], ALU.add), inc="dve")
                em.wait("scalar", "dve", v_b)
                v_s_free[kb] = em.op("scalar", lambda E, sb=sb, kt=kt: E.activation(
                    pT3[:, kt, :], pb[sb][:, 0:W], AF.Exp, scale=float(cfg.SCALING)),
                    inc="act")
                if qgen is not None:
                    for _ in range(2):
                        if next(qgen, "end") == "end":
                            qgen = None
                            break
            em.wait("tensor", "act", em.c["act"])
            for qi in range(QB):
                if qgen is not None:
                    for _ in range(4):
                        if next(qgen, "end") == "end":
                            qgen = None
                            break
                obk = 6 + (qi // 2)
                ocol = (qi % 2) * 256
                oap = pb[obk][:, ocol:ocol + 129]
                va = None
                for i_k, kt in enumerate(kts):
                    va = mm(oap, pT3[:, kt, qi * 128:(qi + 1) * 128],
                            vx3[:, kt, 0:129], i_k == 0, i_k == len(kts) - 1,
                            inc=("pe" if i_k == len(kts) - 1 else None))
                em.wait("vector", "pe", va)
                v_r = em.op("vector", lambda E, obk=obk, ocol=ocol: E.reciprocal(
                    rcp[:], pb[obk][:, ocol + 128:ocol + 129]), inc="dve")
                em.wait("scalar", "dve", v_r)
                em.wait("scalar", "pe", v_osb_free[qi % 2])
                v_n = em.op("scalar", lambda E, obk=obk, ocol=ocol, qi=qi: E.activation(
                    osb[qi % 2][:], pb[obk][:, ocol:ocol + 128], AF.Identity,
                    scale=rcp[:]), inc="act")
                tb2 = 2 + (qi % 4)
                em.wait("tensor", "act", v_n)
                em.wait("tensor", "act", v_s_free[qi % 4])
                em.wait("tensor", "dve", v_sb_evfree[qi % 4])
                v_tr = mm(pb[tb2][:, 0:128], osb[qi % 2][:], ident[:], True, True, inc="pe")
                v_osb_free[qi % 2] = v_tr
                em.wait("vector", "pe", v_tr)
                v_sb_evfree[qi % 4] = em.op(
                    "vector", lambda E, h=h, qb=qb, qi=qi, tb2=tb2:
                    E.tensor_copy(oT3[:, h, qb * W + qi * 128:qb * W + (qi + 1) * 128],
                                  pb[tb2][:, 0:128]), inc="dve")
        if qgen is not None:
            for _ in qgen:
                pass
        if h + 1 < HN:
            q_evacs(h + 1, vq_next)
        v_pt_free = em.c["pe"]
        v_qh_free[hb] = em.c["pe"]

    em.barrier()

    em.phase('P3')
    v_wc_free = [0, 0]
    v_wc_evac = [0, 0, 0, 0]
    for oc in range(OCN):
        ob = oc % 2
        em.wait("gpsimd", "pe", v_wc_free[ob])
        v_w = em.dma("gpsimd", f"wc{ob}", r128(wcb[ob]), a_wc[oc])
        em.wait("tensor", f"wc{ob}", v_w)
        for tb in range(2):
            bk = (oc * 2 + tb) % 4
            em.wait("tensor", "dve", v_wc_evac[bk])
            vm = None
            for ac in range(HN):
                vm = mm(pb[bk][:, 0:W], wcb[ob][:, ac * 128:(ac + 1) * 128],
                        oT3[:, ac, tb * W:(tb + 1) * W], ac == 0, ac == HN - 1,
                        inc=("pe" if ac == HN - 1 else None))
            em.wait("vector", "pe", vm)
            v_wc_evac[bk] = em.op("vector", lambda E, oc=oc, tb=tb, bk=bk: E.tensor_copy(
                xtT3[:, oc, tb * W:(tb + 1) * W], pb[bk][:, 0:W]), inc="dve")
        v_wc_free[ob] = em.c["pe"]

    em.barrier()

    NB = max(1, DC // 4)

    def ln_row_stats_apply(dst, vpre=None):
        vprev = None
        for j in range(NB):
            if vprev is not None:
                em.wait("scalar", "act", vprev)
            vprev = em.op("scalar", lambda E, j=j: E.activation(
                dst[:, j * 512:(j + 1) * 512], pb[j][:, 0:512], AF.Square,
                accum_out=sq8[:, j:j + 1]), inc="act")
        v_a = vprev
        em.wait("vector", "act", v_a)
        w2 = em.op("vector", lambda E: E.reduce_sum(sqt[:], sq8[:, 0:NB], axis=AX.X),
                   inc="dve")
        w5 = em.op("vector", lambda E: E.tensor_tensor(tt1[:], m_t[:], m_t[:], ALU.mult),
                   inc="dve")
        em.wait("vector", "dve", w2)
        w4 = em.op("vector", lambda E: E.tensor_scalar(v_t[:], sqt[:], 1.0 / E_DIM, None,
                                                       ALU.mult), inc="dve")
        em.wait("vector", "dve", max(w4, w5))
        w6 = em.op("vector", lambda E: E.tensor_tensor(v_t[:], v_t[:], tt1[:], ALU.subtract),
                   inc="dve")
        em.wait("vector", "dve", w6)
        v_b = em.op("vector", lambda E: E.tensor_scalar(v_t[:], v_t[:], float(EPS), None,
                                                        ALU.add), inc="dve")
        em.wait("scalar", "dve", v_b)
        u1 = em.op("scalar", lambda E: E.activation(tt1[:], v_t[:], AF.Ln), inc="act")
        em.wait("scalar", "act", u1)
        v_c = em.op("scalar", lambda E: E.activation(rstd[:], tt1[:], AF.Exp, scale=-0.5),
                    inc="act")
        em.wait("vector", "act", v_c)
        w7 = em.op("vector", lambda E: E.tensor_scalar(tt1[:], m_t[:], -1.0, None, ALU.mult),
                   inc="dve")
        em.wait("vector", "dve", w7)
        v_d = em.op("vector", lambda E: E.tensor_tensor(negms[:], tt1[:], rstd[:], ALU.mult),
                    inc="dve")
        em.wait("scalar", "dve", v_d)
        v_e = None
        for j in range(NB):
            if v_e is not None:
                em.wait("scalar", "act", v_e)
            v_e = em.op("scalar", lambda E, j=j: E.activation(
                dst[:, j * 512:(j + 1) * 512], pb[j][:, 0:512], AF.Identity,
                scale=rstd[:], bias=negms[:]), inc="act")
        return v_e

    em.phase('P4')
    v_row_free = 0
    v_x1w = 0
    v_tp4 = [0, 0]
    v_t1_free = 0
    for ti in range(NOWN):
        bi = ti % 2
        tb = ti // QB
        tl = ti % QB
        v_x = em.dma("sync", f"x4{bi}", xs4[bi][:], a_xo[ti * 128:(ti + 1) * 128, :])
        em.wait("tensor", "act", v_row_free)
        em.wait("tensor", "dve", v_tp4[0])
        vm = None
        for dc in range(DC):
            vm = mm(pb[0][:, 0:1], xtT3[:, dc, ti * 128:(ti + 1) * 128], mcol[:],
                    dc == 0, dc == DC - 1, inc=("pe" if dc == DC - 1 else None))
        em.wait("vector", "pe", vm)
        v_me = em.op("vector", lambda E: E.tensor_copy(m_t[:], pb[0][:, 0:1]), inc="dve")
        em.wait("tensor", "dve", v_me)
        em.wait("tensor", "dve", v_tp4[1])
        vq = None
        for dc in range(DC):
            vq = mm(pb[dc // 4][:, (dc % 4) * 128:((dc % 4) + 1) * 128],
                    xtT3[:, dc, ti * 128:(ti + 1) * 128], ident[:],
                    True, True, inc=("pe" if dc == DC - 1 else None))
        em.wait("scalar", "pe", vq)
        em.wait("scalar", "dve", v_t1_free)
        v_apply = ln_row_stats_apply(t1, vpre=("pe", vq))
        em.wait("vector", "act", v_apply)
        em.wait("vector", f"x4{bi}", v_x)
        v_add = em.op("vector", lambda E, bi=bi: E.tensor_tensor(
            x1f[:], t1[:], xs4[bi][:], ALU.add), inc="dve")
        v_t1_free = v_add
        em.wait("vector", "dve", v_add)
        em.wait("vector", "x1w", v_x1w)
        v_c1 = em.op("vector", lambda E: E.tensor_copy(x1b[:], x1f[:]), inc="dve")
        em.wait("sync", "dve", v_c1)
        v_x1w = em.dma("sync", "x1w", d_x1[ti * 128:(ti + 1) * 128, :], x1b[:])
        em.wait("scalar", "dve", v_c1)
        v_ap2, _ = ln_stats_apply(x1f, t1)
        em.wait("tensor", "act", v_ap2)
        for q in range(DC // 4):
            bk = q % 2
            em.wait("tensor", "dve", v_tp4[bk])
            vq = None
            for j in range(4):
                dc = q * 4 + j
                vq = mm(pb[bk][:, j * 128:(j + 1) * 128],
                        t1[:, dc * 128:(dc + 1) * 128], ident[:],
                        True, True, inc=("pe" if j == 3 else None))
            em.wait("vector", "pe", vq)
            v_tp4[bk] = em.op(
                "vector", lambda E, q=q, bk=bk, tb=tb, tl=tl:
                E.tensor_copy(x4T3[tb][:, q * 4:q * 4 + 4, tl * 128:(tl + 1) * 128],
                              pb[bk][:].rearrange("p (d f) -> p d f", f=128)),
                inc="dve")
        v_row_free = v_apply

    em.barrier()
    v_x1_done = em.c["x1w"]

    em.phase('P5')
    gs_ap = sc3[:, 0:1]
    for tb in range(2):
        for s in ("out0", "out1"):
            em.wait("gpsimd", s, em.c.get(s, 0))
            em.wait("sync", s, em.c.get(s, 0))
            em.wait("vector", s, em.c.get(s, 0))
        v_g_free = [0, 0]
        v_u_free = [0, 0]
        v_wbuf = [em.c["pe"], em.c["pe"]]
        for ic in range(ICN):
            ib = ic % 2
            em.wait("gpsimd", "pe", v_wbuf[ib])
            HDC = DC // 2
            v_wg0 = em.dma("gpsimd", f"gw{ib}a", r128(gwb[ib])[:, 0:HDC, :],
                           a_gw[ic, :, 0:HDC, :])
            v_wg1 = em.dma("gpsimd", f"gw{ib}b", r128(gwb[ib])[:, HDC:DC, :],
                           a_gw[ic, :, HDC:DC, :])
            v_wu0 = em.dma("gpsimd", f"uw{ib}a", r128(uwb[ib])[:, 0:HDC, :],
                           a_uw[ic, :, 0:HDC, :])
            v_wu1 = em.dma("gpsimd", f"uw{ib}b", r128(uwb[ib])[:, HDC:DC, :],
                           a_uw[ic, :, HDC:DC, :])
            em.wait("tensor", f"gw{ib}a", v_wg0)
            em.wait("tensor", f"gw{ib}b", v_wg1)
            em.wait("tensor", f"uw{ib}a", v_wu0)
            em.wait("tensor", f"uw{ib}b", v_wu1)
            em.wait("tensor", "act", v_g_free[ib])
            vg = None
            for dc in range(DC):
                vg = mm(pb[ib][:, 0:W], gwb[ib][:, dc * 128:(dc + 1) * 128],
                        x4T3[tb][:, dc, :], dc == 0, dc == DC - 1,
                        inc=("pe" if dc == DC - 1 else None))
            em.wait("tensor", "dve", v_u_free[ib])
            vu = None
            for dc in range(DC):
                vu = mm(pb[2 + ib][:, 0:W], uwb[ib][:, dc * 128:(dc + 1) * 128],
                        x4T3[tb][:, dc, :], dc == 0, dc == DC - 1,
                        inc=("pe" if dc == DC - 1 else None))
            v_wbuf[ib] = vu
            em.wait("scalar", "pe", vg)
            em.wait("scalar", "dve", v_u_free[ib])
            v_g_free[ib] = em.op("scalar", lambda E, ib=ib: E.activation(
                hg[ib][:], pb[ib][:, 0:W], AF.Silu, scale=gs_ap), inc="act")
            em.wait("vector", "pe", vu)
            em.wait("vector", "act", v_g_free[ib])
            v_u_free[ib] = em.op("vector", lambda E, ib=ib, ic=ic: E.tensor_tensor(
                hT3[:, ic, :], pb[2 + ib][:, 0:W], hg[ib][:], ALU.mult), inc="dve")
        em.wait("tensor", "dve", em.c["dve"])
        v_dwbuf = [em.c["pe"], em.c["pe"]]
        v_x5_free = [0, 0]
        for oc in range(OCN):
            ob = oc % 2
            em.wait("gpsimd", "pe", v_dwbuf[ob])
            HIC = ICN // 2
            dwb3 = dwb[ob][:].rearrange("p (i f) -> p i f", f=128)
            v_wd0 = em.dma("gpsimd", f"dw{ob}a", dwb3[:, 0:HIC, :], a_dw[oc, :, 0:HIC, :])
            v_wd1 = em.dma("gpsimd", f"dw{ob}b", dwb3[:, HIC:ICN, :],
                           a_dw[oc, :, HIC:ICN, :])
            em.wait("tensor", f"dw{ob}a", v_wd0)
            em.wait("tensor", f"dw{ob}b", v_wd1)
            em.wait("tensor", "act", v_x5_free[ob])
            vd = None
            for ic in range(ICN):
                vd = mm(pb[ob][:, 0:W], dwb[ob][:, ic * 128:(ic + 1) * 128],
                        hT3[:, ic, :], ic == 0, ic == ICN - 1,
                        inc=("pe" if ic == ICN - 1 else None))
            v_dwbuf[ob] = vd
            em.wait("scalar", "pe", vd)
            v_x5_free[ob] = em.op("scalar", lambda E, ob=ob, oc=oc, tb=tb: E.activation(
                x5T3[tb][:, oc, :], pb[ob][:, 0:W], AF.Identity, scale=usds[:]), inc="act")
        em.wait("tensor", "act", em.c["act"])
        for s in ("dw0", "dw1"):
            em.wait("sync", s, em.c.get(s, 0))
            em.wait("vector", s, em.c.get(s, 0))
            em.wait("scalar", s, em.c.get(s, 0))
        em.wait("sync", "pe", em.c["pe"])
        v_row6 = 0
        v_ob_free = [0, 0]
        v_x1r_free = 0
        v_t5_free = 0
        for tl in range(QB):
            ti = tb * QB + tl
            bi = tl % 2
            em.wait("sync", "x1w", v_x1_done)
            em.wait("sync", "dve", v_x1r_free)
            v_xr = em.dma("sync", "x1r", x1r[:], d_x1[ti * 128:(ti + 1) * 128, :])
            em.wait("tensor", "act", v_row6)
            vm = None
            for dc in range(DC):
                vm = mm(pb[0][:, 0:1], x5T3[tb][:, dc, tl * 128:(tl + 1) * 128],
                        mcol[:], dc == 0, dc == DC - 1,
                        inc=("pe" if dc == DC - 1 else None))
            em.wait("vector", "pe", vm)
            v_me = em.op("vector", lambda E: E.tensor_copy(m_t[:], pb[0][:, 0:1]),
                         inc="dve")
            em.wait("tensor", "dve", v_me)
            vq = None
            for dc in range(DC):
                vq = mm(pb[dc // 4][:, (dc % 4) * 128:((dc % 4) + 1) * 128],
                        x5T3[tb][:, dc, tl * 128:(tl + 1) * 128], ident[:],
                        True, True, inc=("pe" if dc == DC - 1 else None))
            em.wait("scalar", "pe", vq)
            em.wait("scalar", "dve", v_t5_free)
            v_row6 = ln_row_stats_apply(t5, vpre=("pe", vq))
            em.wait("vector", "act", v_row6)
            em.wait("vector", "x1r", v_xr)
            em.wait("vector", f"out{bi}", v_ob_free[bi])
            v_o = em.op("vector", lambda E, bi=bi: E.tensor_tensor(
                outb[bi][:], t5[:], x1r[:], ALU.add), inc="dve")
            v_x1r_free = v_o
            v_t5_free = v_o
            em.wait("sync", "dve", v_o)
            v_ob_free[bi] = em.dma("sync", f"out{bi}", a_out[ti * 128:(ti + 1) * 128, :],
                                   outb[bi][:])

    for s in ("out0", "out1"):
        em.wait("sync", s, em.c.get(s, 0))
    em.barrier()

    import json as _json
    _json.dump({"phases": em.phases,
                "n_mm": getattr(em, "n_mm", 0)},
               open("/tmp/phase_map.json", "w"))
    with nc.Block() as block:
        block.sync(lambda E: [f(E) for f in em.q["sync"]])
        block.tensor(lambda E: [f(E) for f in em.q["tensor"]])
        block.scalar(lambda E: [f(E) for f in em.q["scalar"]])
        block.vector(lambda E: [f(E) for f in em.q["vector"]])
        block.gpsimd(lambda E: [f(E) for f in em.q["gpsimd"]])

    return nc



def tile_weights(w_qkv, w_c, gw, uw, dw, cfg):
    HN, DC, ICN = cfg.HN, cfg.DC, cfg.ICN
    wqkvT = np.ascontiguousarray(w_qkv.reshape(DC, 128, HN + 2, 128).transpose(2, 1, 0, 3))
    wcT = np.ascontiguousarray(w_c.reshape(HN, 128, DC, 128).transpose(2, 1, 0, 3))
    gwT = np.ascontiguousarray(gw.reshape(ICN, 128, DC, 128).transpose(0, 3, 2, 1))
    uwT = np.ascontiguousarray(uw.reshape(ICN, 128, DC, 128).transpose(0, 3, 2, 1))
    dwT = np.ascontiguousarray(dw.reshape(DC, 128, ICN, 128).transpose(0, 3, 2, 1))
    return wqkvT, wcT, gwT, uwT, dwT


def own_rows(cfg, parity):
    idx = []
    for qt in cfg.own_qts(parity):
        idx.extend(range(qt * 128, (qt + 1) * 128))
    return np.array(idx)



def kernel(L0_attn_hidden_states, L0_attn_w_qkv, L0_attn_w_c,
           L0_ffn_gate_w, L0_ffn_gate_s, L0_ffn_up_w, L0_ffn_up_s,
           L0_ffn_down_w, L0_ffn_down_s):
    cfg = Cfg(NT=16, DC=32, HN=32, ICN=86)
    x = np.ascontiguousarray(np.asarray(L0_attn_hidden_states))
    w_qkv = np.ascontiguousarray(np.asarray(L0_attn_w_qkv))
    w_c = np.ascontiguousarray(np.asarray(L0_attn_w_c))
    gw = np.ascontiguousarray(np.asarray(L0_ffn_gate_w))
    uw = np.ascontiguousarray(np.asarray(L0_ffn_up_w))
    dw = np.ascontiguousarray(np.asarray(L0_ffn_down_w))
    gs = np.asarray(L0_ffn_gate_s)
    us = np.asarray(L0_ffn_up_s)
    ds = np.asarray(L0_ffn_down_s)
    B = x.shape[0]
    n_cores = 2 * B

    wqkvT, wcT, gwT, uwT, dwT = tile_weights(w_qkv, w_c, gw, uw, dw, cfg)
    scal3 = np.ascontiguousarray(np.broadcast_to(
        np.array([gs[0], us[0], ds[0]], dtype=np.float32), (128, 3)))
    in_maps = []
    for c in range(n_cores):
        b, p = c // 2, c % 2
        in_maps.append({
            "xb": np.ascontiguousarray(x[b]),
            "xown": np.ascontiguousarray(x[b][own_rows(cfg, p)]),
            "wqkvT": wqkvT, "wcT": wcT, "gwT": gwT, "uwT": uwT, "dwT": dwT,
            "rbase": make_rbase(cfg, p), "scal3": scal3,
        })
    nc = build_program(cfg)
    res = run_bass_kernel_spmd(nc, in_maps, core_ids=list(range(n_cores)))
    out = np.zeros((B, cfg.S, cfg.E), dtype=np.float32)
    for c in range(n_cores):
        b, p = c // 2, c % 2
        out[b][own_rows(cfg, p)] = res.results[c]["out"]
    return out
